# version 02
# baseline (speedup 1.0000x reference)
import numpy as np
import ml_dtypes
from contextlib import ExitStack

import concourse.bass as bass
import concourse.mybir as mybir
from concourse.bass_utils import run_bass_kernel_spmd

F32 = mybir.dt.float32
BF16 = mybir.dt.bfloat16
I32 = mybir.dt.int32
ALU = mybir.AluOpType
ACTF = mybir.ActivationFunctionType

NC = 8
D = 2048
L = 8192
N = 64
HID = 5632
DSL = D // NC
TL = L // NC
CH = 512
NCH = L // CH
TT = 1024
NTT = L // TT
HC = HID // 128
CH2 = 512
EPS = 1e-5
BF = ml_dtypes.bfloat16
DEBUG = False


class Emit:

    def __init__(self, nc, block, sems):
        self.nc = nc
        self.block = block
        self.sem = sems
        self.cnt = {k: 0 for k in sems}

    def dma(self, queue, out, in_, sem="q"):
        sec = getattr(self.block, queue)

        @sec
        def _(s):
            s.dma_start(out=out, in_=in_).then_inc(self.sem[sem], 16)
        self.cnt[sem] += 16
        return self.cnt[sem]

    def act(self, fn):
        @self.block.scalar
        def _(s):
            fn(s).then_inc(self.sem["a"], 1)
        self.cnt["a"] += 1
        return self.cnt["a"]

    def vec(self, fn):
        @self.block.vector
        def _(s):
            fn(s).then_inc(self.sem["v"], 1)
        self.cnt["v"] += 1
        return self.cnt["v"]

    def pe(self, fn):
        @self.block.tensor
        def _(s):
            fn(s).then_inc(self.sem["p"], 1)
        self.cnt["p"] += 1
        return self.cnt["p"]

    def wait(self, eng, sem, val):
        if val <= 0:
            return
        sec = getattr(self.block, eng)
        h = self.sem[sem]

        @sec
        def _(s):
            s.wait_ge(h, val)



def build_scan():
    nc = bass.Bass("TRN2", target_bir_lowering=False, debug=False)

    xT = nc.dram_tensor("xT", [D, L], F32, kind="ExternalInput").ap()
    xTo = nc.dram_tensor("xTo", [DSL, L], F32, kind="ExternalInput").ap()
    PT = nc.dram_tensor("PT", [D, 2 * 128], F32, kind="ExternalInput").ap()
    nw = nc.dram_tensor("nw", [D, 1], F32, kind="ExternalInput").ap()
    nb = nc.dram_tensor("nb", [D, 1], F32, kind="ExternalInput").ap()
    WdtT = nc.dram_tensor("WdtT", [128, DSL], F32, kind="ExternalInput").ap()
    dtb = nc.dram_tensor("dtb", [DSL, 1], F32, kind="ExternalInput").ap()
    Ao = nc.dram_tensor("Ao", [DSL, N], F32, kind="ExternalInput").ap()
    nwo = nc.dram_tensor("nwo", [DSL, 1], F32, kind="ExternalInput").ap()
    nbo = nc.dram_tensor("nbo", [DSL, 1], F32, kind="ExternalInput").ap()
    y_out = nc.dram_tensor("y_out", [DSL, L], BF16, kind="ExternalOutput").ap()
    bcd = nc.dram_tensor("bcd", [128, L], BF16,
                         kind="ExternalOutput" if DEBUG else "Internal").ap()
    if DEBUG:
        dbg_pd = nc.dram_tensor("dbg_pd", [128, L], BF16, kind="ExternalOutput").ap()
        dbg_m1 = nc.dram_tensor("dbg_m1", [128, CH], F32, kind="ExternalOutput").ap()
        dbg_raw = nc.dram_tensor("dbg_raw", [128, CH], F32, kind="ExternalOutput").ap()
        dbg_pwt = nc.dram_tensor("dbg_pwt", [128, 16 * 256], BF16, kind="ExternalOutput").ap()
        dbg_b2 = nc.dram_tensor("dbg_b2", [128, 2], F32, kind="ExternalOutput").ap()
        dbg_wdt = nc.dram_tensor("dbg_wdt", [128, 256], BF16, kind="ExternalOutput").ap()
        dbg_xbf = nc.dram_tensor("dbg_xbf", [128, CH], BF16, kind="ExternalOutput").ap()
        dbg_rbc = nc.dram_tensor("dbg_rbc", [128, CH], F32, kind="ExternalOutput").ap()
        dbg_mrbc = nc.dram_tensor("dbg_mrbc", [128, CH], F32, kind="ExternalOutput").ap()
        dbg_xn = nc.dram_tensor("dbg_xn", [128, 2 * L], BF16, kind="ExternalOutput").ap()
        dbg_dt = nc.dram_tensor("dbg_dt", [128, 2 * TT], F32, kind="ExternalOutput").ap()

    ctx = ExitStack()
    en = ctx.enter_context
    with ctx:
        s_PwT = en(nc.sbuf_tensor("s_PwT", [128, 16 * 256], BF16))
        s_WdtTb = en(nc.sbuf_tensor("s_WdtTb", [128, DSL], BF16))
        s_negK1 = en(nc.sbuf_tensor("s_negK1", [128, 2], F32))
        s_K2bc = en(nc.sbuf_tensor("s_K2bc", [128, 1], F32))
        s_bias2 = en(nc.sbuf_tensor("s_bias2", [128, 2], F32))
        s_A = en(nc.sbuf_tensor("s_A", [128, 2 * N], F32))
        s_nwo = en(nc.sbuf_tensor("s_nwo", [128, 2], F32))
        s_nbo = en(nc.sbuf_tensor("s_nbo", [128, 2], F32))
        s_ones = en(nc.sbuf_tensor("s_ones", [128, 1], BF16))
        s_onesr = en(nc.sbuf_tensor("s_onesr", [1, 128], F32))
        s_pdelta = en(nc.sbuf_tensor("s_pdelta", [128, L], BF16))
        s_xn = en(nc.sbuf_tensor("s_xn", [128, 2 * L], BF16))
        s_carry = en(nc.sbuf_tensor("s_carry", [128, 2 * N], BF16))
        sems = {k: en(nc.semaphore("s1_" + k)) for k in ("q", "q2", "a", "v", "p")}
        block = en(nc.Block())
        E = Emit(nc, block, sems)

        pctx = ExitStack()
        pe_ = pctx.enter_context
        s_xt = pe_(nc.sbuf_tensor("s_xt", [128, 2 * CH], F32))
        s_xbf = pe_(nc.sbuf_tensor("s_xbf", [128, 2 * CH], BF16))
        s_xsq = pe_(nc.sbuf_tensor("s_xsq", [128, 2 * CH], BF16))
        s_row = pe_(nc.sbuf_tensor("s_row", [1, 3 * CH], F32))
        s_rbc = pe_(nc.sbuf_tensor("s_rbc", [128, CH], F32))
        s_mrbc = pe_(nc.sbuf_tensor("s_mrbc", [128, CH], F32))
        s_m1 = pe_(nc.sbuf_tensor("s_m1", [128, CH], F32))
        s_bcst = pe_(nc.sbuf_tensor("s_bcst", [128, 2 * CH], BF16))
        s_xto = pe_(nc.sbuf_tensor("s_xto", [128, CH], F32))
        s_ptmp = pe_(nc.sbuf_tensor("s_ptmp", [128, 256], F32))
        s_ptbf = pe_(nc.sbuf_tensor("s_ptbf", [128, 256], BF16))
        s_cw = pe_(nc.sbuf_tensor("s_cw", [128, 2], F32))
        s_k2bf = pe_(nc.sbuf_tensor("s_k2bf", [128, 1], BF16))
        p_sum = pe_(nc.psum_tensor("p_sum", [1, CH], F32))
        p_sq = pe_(nc.psum_tensor("p_sq", [1, CH], F32))
        p_p0 = pe_(nc.psum_tensor("p_p0", [128, CH], F32))
        p_p1 = pe_(nc.psum_tensor("p_p1", [128, CH], F32))
        p_rbc = pe_(nc.psum_tensor("p_rbc", [128, CH], F32))
        p_mrbc = pe_(nc.psum_tensor("p_mrbc", [128, CH], F32))

        E.vec(lambda v: v.memset(s_ones[:], 1.0))
        E.vec(lambda v: v.memset(s_onesr[:], 1.0))
        E.dma("sync", s_xt[:, 0:N], Ao[0:128, :])
        E.dma("sync", s_xt[:, N:2 * N], Ao[128:256, :])
        E.dma("sync", s_bias2[:, 0:1], dtb[0:128, :])
        E.dma("sync", s_bias2[:, 1:2], dtb[128:256, :])
        E.dma("sync", s_nwo[:, 0:1], nwo[0:128, :])
        E.dma("sync", s_nwo[:, 1:2], nwo[128:256, :])
        E.dma("sync", s_nbo[:, 0:1], nbo[0:128, :])
        E.dma("sync", s_nbo[:, 1:2], nbo[128:256, :])
        E.wait("scalar", "q", E.cnt["q"])
        E.act(lambda s: s.activation(out=s_A[:], in_=s_xt[:, 0:2 * N], func=ACTF.Exp))
        E.wait("vector", "a", E.cnt["a"])
        E.vec(lambda v: v.tensor_scalar_mul(s_A[:], s_A[:], -1.0))

        for dg in range(16):
            E.wait("sync", "v", E.cnt["v"] if dg else 0)
            E.dma("sync", s_ptmp[:], PT[dg * 128:(dg + 1) * 128, :])
            E.dma("sync", s_cw[:, 0:1], nw[dg * 128:(dg + 1) * 128, :])
            E.dma("sync", s_cw[:, 1:2], nb[dg * 128:(dg + 1) * 128, :])
            E.wait("scalar", "q", E.cnt["q"])
            E.wait("scalar", "p", E.cnt["p"])
            E.act(lambda s: s.activation(out=s_ptbf[:], in_=s_ptmp[:], func=ACTF.Copy))
            E.wait("vector", "a", E.cnt["a"])
            E.vec(lambda v, dg=dg: v.tensor_scalar(
                s_PwT[:, dg * 256:(dg + 1) * 256], s_ptmp[:], s_cw[:, 0:1], None, ALU.mult))
            E.vec(lambda v: v.tensor_copy(s_k2bf[:], s_cw[:, 1:2]))
            E.wait("tensor", "v", E.cnt["v"])
            for h in range(2):
                E.pe(lambda s, dg=dg, h=h: s.matmul(
                    p_p0[:, h:h + 1], s_PwT[:, dg * 256 + h * 128:dg * 256 + (h + 1) * 128],
                    s_ones[:], start=(dg == 0 and h == 0), stop=(dg == 15)))
                E.pe(lambda s, dg=dg, h=h: s.matmul(
                    p_p1[:, h:h + 1], s_ptbf[:, h * 128:(h + 1) * 128],
                    s_k2bf[:], start=(dg == 0 and h == 0), stop=(dg == 15)))
        E.wait("vector", "p", E.cnt["p"])
        E.vec(lambda v: v.tensor_scalar(s_negK1[:], p_p0[:, 0:2], -1.0, None, ALU.mult))
        E.vec(lambda v: v.tensor_copy(s_K2bc[:], p_p1[:, 1:2]))
        E.vec(lambda v: v.tensor_copy(s_k2bf[:], p_p1[:, 0:1]))
        E.wait("sync", "v", E.cnt["v"])
        E.dma("sync", s_ptmp[:], WdtT)
        E.wait("scalar", "q", E.cnt["q"])
        E.wait("scalar", "v", E.cnt["v"])
        E.act(lambda s: s.activation(out=s_WdtTb[:], in_=s_ptmp[:], func=ACTF.Copy))
        E.wait("tensor", "a", E.cnt["a"])
        E.wait("tensor", "v", E.cnt["v"])
        for h in range(2):
            E.pe(lambda s, h=h: s.matmul(
                p_p0[:, 4 + h:5 + h], s_WdtTb[:, h * 128:(h + 1) * 128],
                s_k2bf[:], start=True, stop=True))
        E.wait("scalar", "p", E.cnt["p"])
        E.act(lambda s: s.activation(out=s_cw[:, 0:2], in_=p_p0[:, 4:6], func=ACTF.Copy))
        E.wait("vector", "a", E.cnt["a"])
        E.vec(lambda v: v.tensor_add(s_bias2[:], s_bias2[:], s_cw[:, 0:2]))

        xdone = {}
        bcw = {}
        for c in range(NCH):
            col = slice(c * CH, (c + 1) * CH)
            E.wait("tensor", "a", E.cnt["a"])
            E.wait("tensor", "v", E.cnt["v"])
            for dg in range(16):
                idx = c * 16 + dg
                b = dg % 2
                xt = s_xt[:, b * CH:(b + 1) * CH]
                xbf = s_xbf[:, b * CH:(b + 1) * CH]
                xsq = s_xsq[:, b * CH:(b + 1) * CH]
                if idx >= 2:
                    E.wait("sync", "p", xdone[idx - 2])
                E.dma("sync", xt, xT[dg * 128:(dg + 1) * 128, col])
                E.wait("scalar", "q", E.cnt["q"])
                E.act(lambda s, xbf=xbf, xt=xt: s.activation(out=xbf, in_=xt, func=ACTF.Copy))
                E.act(lambda s, xsq=xsq, xt=xt: s.activation(out=xsq, in_=xt, func=ACTF.Square))
                E.wait("tensor", "a", E.cnt["a"])
                st, sp = (dg == 0), (dg == 15)
                E.pe(lambda s, xbf=xbf, st=st, sp=sp: s.matmul(p_sum[:], s_ones[:], xbf,
                                                               start=st, stop=sp))
                E.pe(lambda s, xsq=xsq, st=st, sp=sp: s.matmul(p_sq[:], s_ones[:], xsq,
                                                               start=st, stop=sp))
                E.pe(lambda s, dg=dg, xbf=xbf, st=st, sp=sp: s.matmul(
                    p_p0[:], s_PwT[:, dg * 256:dg * 256 + 128], xbf, start=st, stop=sp))
                xdone[idx] = E.pe(lambda s, dg=dg, xbf=xbf, st=st, sp=sp: s.matmul(
                    p_p1[:], s_PwT[:, dg * 256 + 128:(dg + 1) * 256], xbf, start=st, stop=sp))
                if DEBUG and c == 0 and dg == 15:
                    E.wait("scalar", "p", E.cnt["p"])
                    E.dma("scalar", dbg_xbf, xbf, sem="q2")
                    E.act(lambda s: s.activation(out=s_rbc[:], in_=p_p0[:], func=ACTF.Copy))
                    E.wait("scalar", "a", E.cnt["a"])
                    E.dma("scalar", dbg_raw, s_rbc[:], sem="q2")
                    E.wait("scalar", "q2", E.cnt["q2"])
            E.wait("scalar", "p", E.cnt["p"])
            mu = s_row[:, 0:CH]
            bb_ = s_row[:, CH:2 * CH]
            cc_ = s_row[:, 2 * CH:3 * CH]
            E.act(lambda s: s.activation(out=mu, in_=p_sum[:], func=ACTF.Copy, scale=np.float32(1.0 / D)))
            E.act(lambda s: s.activation(out=bb_, in_=p_sq[:], func=ACTF.Copy, scale=np.float32(1.0 / D)))
            E.wait("vector", "a", E.cnt["a"])
            E.vec(lambda v: v.tensor_mul(cc_, mu, mu))
            E.vec(lambda v: v.tensor_sub(bb_, bb_, cc_))
            E.wait("scalar", "v", E.cnt["v"])
            E.act(lambda s: s.activation(out=bb_, in_=bb_, func=ACTF.Ln, bias=np.float32(EPS)))
            E.act(lambda s: s.activation(out=bb_, in_=bb_, func=ACTF.Exp, scale=np.float32(-0.5)))
            E.wait("vector", "a", E.cnt["a"])
            rr, mr = bb_, cc_
            E.vec(lambda v: v.tensor_mul(mr, mu, rr))
            E.wait("tensor", "v", E.cnt["v"])
            E.pe(lambda s: s.matmul(p_rbc[:], s_onesr[:], rr, start=True, stop=True))
            E.pe(lambda s: s.matmul(p_mrbc[:], s_onesr[:], mr, start=True, stop=True))
            E.wait("scalar", "p", E.cnt["p"])
            E.act(lambda s: s.activation(out=s_rbc[:], in_=p_rbc[:], func=ACTF.Copy))
            E.act(lambda s: s.activation(out=s_mrbc[:], in_=p_mrbc[:], func=ACTF.Copy))
            E.wait("vector", "a", E.cnt["a"])
            E.vec(lambda v: v.tensor_mul(s_m1[:], p_p0[:], s_rbc[:]))
            if DEBUG and c == 0:
                E.wait("scalar", "v", E.cnt["v"])
                E.dma("scalar", dbg_m1, s_m1[:], sem="q2")
                E.dma("scalar", dbg_rbc, s_rbc[:], sem="q2")
                E.dma("scalar", dbg_mrbc, s_mrbc[:], sem="q2")
                E.wait("vector", "q2", E.cnt["q2"])
            E.vec(lambda v, col=col: v.scalar_tensor_tensor(
                out=s_pdelta[:, col], in0=s_mrbc[:], scalar=s_negK1[:, 0:1],
                in1=s_m1[:], op0=ALU.mult, op1=ALU.add))
            bs = s_bcst[:, (c % 2) * CH:((c % 2) + 1) * CH]
            if c >= 2:
                E.wait("vector", "q2", bcw[c - 2])
            E.vec(lambda v: v.tensor_mul(s_m1[:], p_p1[:], s_rbc[:]))
            E.vec(lambda v: v.scalar_tensor_tensor(
                out=s_m1[:], in0=s_mrbc[:], scalar=s_negK1[:, 1:2],
                in1=s_m1[:], op0=ALU.mult, op1=ALU.add))
            E.vec(lambda v, bs=bs: v.tensor_scalar(bs, s_m1[:], s_K2bc[:], None, ALU.add))
            E.wait("scalar", "v", E.cnt["v"])
            bcw[c] = E.dma("scalar", bcd[:, col], bs, sem="q2")
            for d2 in range(2):
                E.wait("scalar", "v", E.cnt["v"])
                E.dma("scalar", s_xto[:], xTo[d2 * 128:(d2 + 1) * 128, col], sem="q2")
                E.wait("vector", "q2", E.cnt["q2"])
                E.vec(lambda v: v.tensor_mul(s_m1[:], s_xto[:], s_rbc[:]))
                E.vec(lambda v: v.scalar_tensor_tensor(
                    out=s_m1[:], in0=s_mrbc[:], scalar=-1.0, in1=s_m1[:],
                    op0=ALU.mult, op1=ALU.add))
                E.vec(lambda v, d2=d2, c=c: v.tensor_scalar(
                    s_xn[:, d2 * L + c * CH:d2 * L + (c + 1) * CH], s_m1[:],
                    s_nwo[:, d2:d2 + 1], s_nbo[:, d2:d2 + 1], ALU.mult, ALU.add))

        for eng in ("sync", "scalar", "vector"):
            E.wait(eng, "q", E.cnt["q"])
            E.wait(eng, "q2", E.cnt["q2"])
        pctx.close()
        p_dt = en(nc.psum_tensor("p_dt", [128, 2 * TT], F32))
        s_dt = en(nc.sbuf_tensor("s_dt", [128, 2 * TT], F32))
        s_dtxt = en(nc.sbuf_tensor("s_dtxt", [128, 2 * TT], BF16))
        s_sp = en(nc.sbuf_tensor("s_sp", [128, TT], F32))
        s_a = en(nc.sbuf_tensor("s_a", [128, 2 * TT], F32))
        s_brep = en(nc.sbuf_tensor("s_brep", [128, 2 * 8 * TT], BF16))
        s_crep = en(nc.sbuf_tensor("s_crep", [128, 2 * 8 * CH], BF16))
        s_u8 = en(nc.sbuf_tensor("s_u8", [128, 8 * TT], BF16))
        s_h8 = en(nc.sbuf_tensor("s_h8", [128, 8 * TT], BF16))
        s_yg = en(nc.sbuf_tensor("s_yg", [128, TT], BF16))
        s_yacc = en(nc.sbuf_tensor("s_yacc", [128, 2 * TT], F32))
        s_yst = en(nc.sbuf_tensor("s_yst", [128, 2 * TT], BF16))

        E.wait("sync", "q2", E.cnt["q2"])
        udone = {}
        ydma = {}
        for t in range(NTT):
            tcol = slice(t * TT, (t + 1) * TT)
            E.wait("tensor", "v", E.cnt["v"])
            E.wait("tensor", "a", E.cnt["a"])
            for d2 in range(2):
                for f in range(2):
                    E.pe(lambda s, d2=d2, f=f, t=t: s.matmul(
                        p_dt[:, d2 * TT + f * CH:d2 * TT + (f + 1) * CH],
                        s_WdtTb[:, d2 * 128:(d2 + 1) * 128],
                        s_pdelta[:, t * TT + f * CH:t * TT + (f + 1) * CH],
                        start=True, stop=True))
            E.wait("scalar", "p", E.cnt["p"])
            for d2 in range(2):
                pdt = p_dt[:, d2 * TT:(d2 + 1) * TT]
                if d2 == 1:
                    E.wait("scalar", "v", E.cnt["v"])
                E.act(lambda s, pdt=pdt, d2=d2: s.activation(
                    out=s_sp[:], in_=pdt, func=ACTF.Abs, bias=s_bias2[:, d2:d2 + 1]))
                E.act(lambda s: s.activation(out=s_sp[:], in_=s_sp[:], func=ACTF.Exp,
                                             scale=np.float32(-1.0)))
                E.act(lambda s: s.activation(out=s_sp[:], in_=s_sp[:], func=ACTF.Ln,
                                             bias=np.float32(1.0)))
                E.wait("vector", "a", E.cnt["a"])
                E.vec(lambda v, pdt=pdt, d2=d2: v.tensor_scalar(
                    s_dt[:, d2 * TT:(d2 + 1) * TT], pdt, s_bias2[:, d2:d2 + 1], 0.0,
                    ALU.add, ALU.max))
                E.vec(lambda v, d2=d2: v.tensor_add(
                    s_dt[:, d2 * TT:(d2 + 1) * TT], s_dt[:, d2 * TT:(d2 + 1) * TT], s_sp[:]))
                E.vec(lambda v, d2=d2, t=t: v.tensor_tensor(
                    out=s_dtxt[:, d2 * TT:(d2 + 1) * TT], in0=s_dt[:, d2 * TT:(d2 + 1) * TT],
                    in1=s_xn[:, d2 * L + t * TT:d2 * L + (t + 1) * TT], op=ALU.mult))
            cv_dt = E.cnt["v"]
            for d2 in range(2):
                E.wait("scalar", "v", cv_dt)
                for g in range(8):
                    j = t * 16 + d2 * 8 + g
                    bb = j % 2
                    brep = s_brep[:, bb * 8 * TT:(bb + 1) * 8 * TT]
                    crep = s_crep[:, bb * 8 * CH:(bb + 1) * 8 * CH]
                    if j >= 2:
                        E.wait("sync", "v", udone[j - 2])
                    E.dma("sync", brep.rearrange("p (n t) -> p n t", n=8),
                          bcd[g * 8:(g + 1) * 8, tcol].unsqueeze(0)
                          .broadcast_to([128, 8, TT]))
                    E.dma("sync", crep.rearrange("p (n t) -> p n t", n=8),
                          bcd[64 + g * 8:64 + (g + 1) * 8, 0:CH].unsqueeze(0)
                          .broadcast_to([128, 8, CH]))
                    E.wait("vector", "q", E.cnt["q"])
                    E.vec(lambda v, brep=brep, d2=d2: v.tensor_tensor(
                        out=s_u8.ap().rearrange("p (n t) -> p n t", n=8),
                        in0=s_dtxt[:, d2 * TT:(d2 + 1) * TT].unsqueeze(1)
                        .broadcast_to([128, 8, TT]),
                        in1=brep.rearrange("p (n t) -> p n t", n=8), op=ALU.mult))
                    for i in range(8):
                        n = g * 8 + i
                        ab = s_a[:, (i % 2) * TT:((i % 2) + 1) * TT]
                        if i >= 2:
                            E.wait("scalar", "v", E.cnt["v"] - 1)
                        E.act(lambda s, ab=ab, d2=d2, n=n: s.activation(
                            out=ab, in_=s_dt[:, d2 * TT:(d2 + 1) * TT], func=ACTF.Exp,
                            scale=s_A[:, d2 * N + n:d2 * N + n + 1]))
                        E.wait("vector", "a", E.cnt["a"])
                        init = 0.0 if t == 0 else s_carry[:, d2 * N + n:d2 * N + n + 1]
                        E.vec(lambda v, ab=ab, i=i, init=init: v.tensor_tensor_scan(
                            out=s_h8[:, i * TT:(i + 1) * TT], data0=ab,
                            data1=s_u8[:, i * TT:(i + 1) * TT], initial=init,
                            op0=ALU.mult, op1=ALU.add))
                    E.vec(lambda v, d2=d2, g=g: v.tensor_copy(
                        out=s_carry[:, d2 * N + g * 8:d2 * N + (g + 1) * 8],
                        in_=s_h8.ap().rearrange("p (n t) -> p n t", n=8)[:, :, TT - 1]))
                    E.vec(lambda v, crep=crep: v.tensor_tensor(
                        out=s_h8.ap().rearrange("p (n r t) -> p n r t", n=8, r=TT // CH),
                        in0=s_h8.ap().rearrange("p (n r t) -> p n r t", n=8, r=TT // CH),
                        in1=crep.rearrange("p (n t) -> p n t", n=8).unsqueeze(2)
                        .broadcast_to([128, 8, TT // CH, CH]), op=ALU.mult))
                    E.vec(lambda v: v.tensor_add(s_h8[:, 0:4 * TT], s_h8[:, 0:4 * TT],
                                                 s_h8[:, 4 * TT:8 * TT]))
                    E.vec(lambda v: v.tensor_add(s_h8[:, 0:2 * TT], s_h8[:, 0:2 * TT],
                                                 s_h8[:, 2 * TT:4 * TT]))
                    if g == 0:
                        udone[j] = E.vec(lambda v, d2=d2: v.tensor_add(
                            s_yacc[:, d2 * TT:(d2 + 1) * TT], s_h8[:, 0:TT],
                            s_h8[:, TT:2 * TT]))
                    else:
                        E.vec(lambda v: v.tensor_add(s_yg[:], s_h8[:, 0:TT],
                                                     s_h8[:, TT:2 * TT]))
                        udone[j] = E.vec(lambda v, d2=d2: v.tensor_add(
                            s_yacc[:, d2 * TT:(d2 + 1) * TT],
                            s_yacc[:, d2 * TT:(d2 + 1) * TT], s_yg[:]))
                if t > 0:
                    E.wait("vector", "q2", ydma[(t - 1, d2)])
                E.vec(lambda v, d2=d2: v.tensor_scalar(
                    s_yst[:, d2 * TT:(d2 + 1) * TT], s_yacc[:, d2 * TT:(d2 + 1) * TT],
                    1.0, None, ALU.mult))
                E.wait("scalar", "v", E.cnt["v"])
                ydma[(t, d2)] = E.dma("scalar", y_out[d2 * 128:(d2 + 1) * 128, tcol],
                                      s_yst[:, d2 * TT:(d2 + 1) * TT], sem="q2")

        if DEBUG:
            E.wait("sync", "v", E.cnt["v"])
            E.dma("sync", dbg_pwt, s_PwT[:])
            E.dma("sync", dbg_b2, s_bias2[:])
            E.dma("sync", dbg_wdt, s_WdtTb[:])
            E.dma("sync", dbg_pd, s_pdelta[:])
            E.dma("sync", dbg_xn, s_xn[:])
            E.dma("sync", dbg_dt, s_dt[:])
        E.wait("gpsimd", "q2", E.cnt["q2"])
        E.wait("gpsimd", "q", E.cnt["q"])
    return nc



def build_ffn():
    nc = bass.Bass("TRN2", target_bir_lowering=False, debug=False)

    yT = nc.dram_tensor("yT", [D, TL], BF16, kind="ExternalInput").ap()
    xTt = nc.dram_tensor("xTt", [D, TL], F32, kind="ExternalInput").ap()
    Dp = nc.dram_tensor("Dp", [D, 1], F32, kind="ExternalInput").ap()
    WoT = nc.dram_tensor("WoT", [D, D], F32, kind="ExternalInput").ap()
    WgT = nc.dram_tensor("WgT", [D, HID], I32, kind="ExternalInput").ap()
    WuT = nc.dram_tensor("WuT", [D, HID], I32, kind="ExternalInput").ap()
    WdT = nc.dram_tensor("WdT", [HID, D], I32, kind="ExternalInput").ap()
    sg = nc.dram_tensor("sg", [HID, 1], F32, kind="ExternalInput").ap()
    su = nc.dram_tensor("su", [HID, 1], F32, kind="ExternalInput").ap()
    sd = nc.dram_tensor("sd", [D, 1], F32, kind="ExternalInput").ap()
    out = nc.dram_tensor("out", [D, TL], F32, kind="ExternalOutput").ap()

    ctx = ExitStack()
    en = ctx.enter_context
    with ctx:
        s_ones = en(nc.sbuf_tensor("f_ones", [128, 1], BF16))
        s_onesr = en(nc.sbuf_tensor("f_onesr", [1, 128], F32))
        s_c128 = en(nc.sbuf_tensor("f_c128", [128, 1], F32))
        s_Dp = en(nc.sbuf_tensor("f_Dp", [128, 16], F32))
        s_sd = en(nc.sbuf_tensor("f_sd", [128, 16], F32))
        s_sg = en(nc.sbuf_tensor("f_sg", [128, HC], F32))
        s_su = en(nc.sbuf_tensor("f_su", [128, HC], F32))
        s_zin = en(nc.sbuf_tensor("f_zin", [128, 16 * CH2], BF16))
        s_xt = en(nc.sbuf_tensor("f_xt", [128, 16 * CH2], F32))
        s_z = en(nc.sbuf_tensor("f_z", [128, 16 * CH2], F32))
        s_zbf = en(nc.sbuf_tensor("f_zbf", [128, 16 * CH2], BF16))
        s_wf = en(nc.sbuf_tensor("f_wf", [128, 2 * 1024], F32))
        s_wb = en(nc.sbuf_tensor("f_wb", [128, 2 * 1024], BF16))
        s_wi = en(nc.sbuf_tensor("f_wi", [128, 2 * 1024], I32))
        s_wib = en(nc.sbuf_tensor("f_wib", [128, 2 * 1024], BF16))
        s_hid = en(nc.sbuf_tensor("f_hid", [128, HC * CH2], BF16))
        s_ut = en(nc.sbuf_tensor("f_ut", [128, CH2], BF16))
        s_row = en(nc.sbuf_tensor("f_row", [1, 8 * CH2], F32))
        s_rbc = en(nc.sbuf_tensor("f_rbc", [128, CH2], F32))
        s_mrbc = en(nc.sbuf_tensor("f_mrbc", [128, CH2], F32))
        s_m1 = en(nc.sbuf_tensor("f_m1", [128, CH2], F32))
        s_sq = en(nc.sbuf_tensor("f_sq", [128, CH2], BF16))
        s_ob = en(nc.sbuf_tensor("f_ob", [128, 2 * CH2], F32))

        p_mm = en(nc.psum_tensor("p_mm", [128, 8 * CH2], F32))
        sems = {k: en(nc.semaphore("s2_" + k)) for k in ("q", "q2", "a", "v", "p")}
        block = en(nc.Block())
        E = Emit(nc, block, sems)

        E.vec(lambda v: v.memset(s_ones[:], 1.0))
        E.vec(lambda v: v.memset(s_onesr[:], 1.0))
        E.vec(lambda v: v.memset(s_c128[:], -128.0))
        nc.const_aps.register(F32, -128.0, s_c128[:])
        E.wait("scalar", "v", E.cnt["v"])
        for dg in range(16):
            E.dma("sync", s_Dp[:, dg:dg + 1], Dp[dg * 128:(dg + 1) * 128, :])
            E.dma("sync", s_sd[:, dg:dg + 1], sd[dg * 128:(dg + 1) * 128, :])
        for hc in range(HC):
            E.dma("sync", s_sg[:, hc:hc + 1], sg[hc * 128:(hc + 1) * 128, :])
            E.dma("sync", s_su[:, hc:hc + 1], su[hc * 128:(hc + 1) * 128, :])

        def ln_rows(p_s, p_q, pb1, pb2):
            mu, e2 = s_row[:, 0:CH2], s_row[:, CH2:2 * CH2]
            mu2, var = s_row[:, 2 * CH2:3 * CH2], s_row[:, 3 * CH2:4 * CH2]
            lnv = s_row[:, 4 * CH2:5 * CH2]
            rr, mr = s_row[:, 5 * CH2:6 * CH2], s_row[:, 6 * CH2:7 * CH2]
            E.act(lambda s: s.activation(out=mu, in_=p_s, func=ACTF.Copy, scale=np.float32(1.0 / D)))
            E.act(lambda s: s.activation(out=e2, in_=p_q, func=ACTF.Copy, scale=np.float32(1.0 / D)))
            E.wait("vector", "a", E.cnt["a"])
            E.vec(lambda v: v.tensor_mul(mu2, mu, mu))
            E.vec(lambda v: v.tensor_sub(var, e2, mu2))
            E.wait("scalar", "v", E.cnt["v"])
            E.act(lambda s: s.activation(out=lnv, in_=var, func=ACTF.Ln, bias=np.float32(EPS)))
            E.act(lambda s: s.activation(out=rr, in_=lnv, func=ACTF.Exp, scale=np.float32(-0.5)))
            E.wait("vector", "a", E.cnt["a"])
            E.vec(lambda v: v.tensor_mul(mr, mu, rr))
            E.wait("tensor", "v", E.cnt["v"])
            E.pe(lambda s: s.matmul(pb1, s_onesr[:], rr, start=True, stop=True))
            E.pe(lambda s: s.matmul(pb2, s_onesr[:], mr, start=True, stop=True))
            E.wait("scalar", "p", E.cnt["p"])
            E.act(lambda s: s.activation(out=s_rbc[:], in_=pb1, func=ACTF.Copy))
            E.act(lambda s: s.activation(out=s_mrbc[:], in_=pb2, func=ACTF.Copy))
            E.wait("vector", "a", E.cnt["a"])

        wdone = {}
        odma = {}
        widx = [0]
        for c in range(TL // CH2):
            col = slice(c * CH2, (c + 1) * CH2)
            E.wait("sync", "v", E.cnt["v"])
            E.wait("scalar", "v", E.cnt["v"])
            for dg in range(16):
                E.dma("sync", s_xt[:, dg * CH2:(dg + 1) * CH2],
                      xTt[dg * 128:(dg + 1) * 128, col])
                E.dma("scalar", s_zbf[:, dg * CH2:(dg + 1) * CH2],
                      yT[dg * 128:(dg + 1) * 128, col], sem="q2")
            E.wait("vector", "q", E.cnt["q"])
            E.wait("vector", "q2", E.cnt["q2"])
            for dg in range(16):
                E.vec(lambda v, dg=dg: v.tensor_scalar(
                    s_m1[:], s_xt[:, dg * CH2:(dg + 1) * CH2], s_Dp[:, dg:dg + 1],
                    None, ALU.mult))
                E.vec(lambda v, dg=dg: v.tensor_add(
                    s_zin[:, dg * CH2:(dg + 1) * CH2], s_m1[:],
                    s_zbf[:, dg * CH2:(dg + 1) * CH2]))
            for eg in range(2):
                if eg == 1:
                    E.wait("tensor", "v", E.cnt["v"])
                for dg in range(16):
                    wi2 = widx[0]
                    widx[0] += 1
                    wf = s_wf[:, (wi2 % 2) * 1024:((wi2 % 2) + 1) * 1024]
                    wb = s_wb[:, (wi2 % 2) * 1024:((wi2 % 2) + 1) * 1024]
                    if wi2 >= 2:
                        E.wait("sync", "p", wdone[wi2 - 2])
                    E.dma("sync", wf, WoT[dg * 128:(dg + 1) * 128,
                                          eg * 1024:(eg + 1) * 1024])
                    E.wait("scalar", "q", E.cnt["q"])
                    E.act(lambda s, wb=wb, wf=wf: s.activation(out=wb, in_=wf,
                                                               func=ACTF.Copy))
                    E.wait("tensor", "a", E.cnt["a"])
                    if eg == 0 and dg == 0:
                        E.wait("tensor", "v", E.cnt["v"])
                    for ec in range(8):
                        last = E.pe(lambda s, wb=wb, ec=ec, dg=dg: s.matmul(
                            p_mm[:, ec * CH2:(ec + 1) * CH2], wb[:, ec * 128:(ec + 1) * 128],
                            s_zin[:, dg * CH2:(dg + 1) * CH2],
                            start=(dg == 0), stop=(dg == 15)))
                    wdone[wi2] = last
                E.wait("vector", "p", E.cnt["p"])
                for ec in range(8):
                    dgl = eg * 8 + ec
                    E.vec(lambda v, ec=ec, dgl=dgl: v.scalar_tensor_tensor(
                        out=s_z[:, dgl * CH2:(dgl + 1) * CH2],
                        in0=p_mm[:, ec * CH2:(ec + 1) * CH2], scalar=1.0,
                        in1=s_xt[:, dgl * CH2:(dgl + 1) * CH2],
                        op0=ALU.mult, op1=ALU.add))
            E.wait("scalar", "v", E.cnt["v"])
            E.wait("tensor", "v", E.cnt["v"])
            for dg in range(16):
                E.wait("scalar", "p", E.cnt["p"])
                E.act(lambda s, dg=dg: s.activation(
                    out=s_zbf[:, dg * CH2:(dg + 1) * CH2],
                    in_=s_z[:, dg * CH2:(dg + 1) * CH2], func=ACTF.Copy))
                E.act(lambda s, dg=dg: s.activation(
                    out=s_sq[:], in_=s_z[:, dg * CH2:(dg + 1) * CH2], func=ACTF.Square))
                E.wait("tensor", "a", E.cnt["a"])
                E.pe(lambda s, dg=dg: s.matmul(p_mm[0:1, 0:CH2], s_ones[:],
                                               s_zbf[:, dg * CH2:(dg + 1) * CH2],
                                               start=(dg == 0), stop=(dg == 15)))
                E.pe(lambda s, dg=dg: s.matmul(p_mm[0:1, CH2:2 * CH2], s_ones[:], s_sq[:],
                                               start=(dg == 0), stop=(dg == 15)))
            E.wait("scalar", "p", E.cnt["p"])
            ln_rows(p_mm[0:1, 0:CH2], p_mm[0:1, CH2:2 * CH2],
                    p_mm[:, 2 * CH2:3 * CH2], p_mm[:, 3 * CH2:4 * CH2])
            for dg in range(16):
                E.vec(lambda v, dg=dg: v.tensor_mul(
                    s_m1[:], s_z[:, dg * CH2:(dg + 1) * CH2], s_rbc[:]))
                E.vec(lambda v, dg=dg: v.tensor_sub(
                    s_z[:, dg * CH2:(dg + 1) * CH2], s_m1[:], s_mrbc[:]))
                E.wait("scalar", "v", E.cnt["v"])
                E.act(lambda s, dg=dg: s.activation(
                    out=s_zbf[:, dg * CH2:(dg + 1) * CH2],
                    in_=s_z[:, dg * CH2:(dg + 1) * CH2], func=ACTF.Copy))
            E.wait("tensor", "a", E.cnt["a"])
            for part, wsrc in (("g", WgT), ("u", WuT)):
                for hcg in range(6):
                    hcs = list(range(hcg * 8, min(HC, hcg * 8 + 8)))
                    for dg in range(16):
                        wi2 = widx[0]
                        widx[0] += 1
                        wi = s_wi[:, (wi2 % 2) * 1024:((wi2 % 2) + 1) * 1024]
                        wib = s_wib[:, (wi2 % 2) * 1024:((wi2 % 2) + 1) * 1024]
                        nh = len(hcs) * 128
                        if wi2 >= 2:
                            E.wait("sync", "p", wdone[wi2 - 2])
                        E.dma("sync", wi[:, 0:nh],
                              wsrc[dg * 128:(dg + 1) * 128, hcg * 1024:hcg * 1024 + nh])
                        E.wait("scalar", "q", E.cnt["q"])
                        E.act(lambda s, wib=wib, wi=wi, nh=nh: s.activation(
                            out=wib[:, 0:nh], in_=wi[:, 0:nh], func=ACTF.Copy, bias=-128.0))
                        E.wait("tensor", "a", E.cnt["a"])
                        if dg == 0:
                            E.wait("tensor", "v", E.cnt["v"])
                        for k in range(len(hcs)):
                            last = E.pe(lambda s, wib=wib, k=k, dg=dg: s.matmul(
                                p_mm[:, k * CH2:(k + 1) * CH2],
                                wib[:, k * 128:(k + 1) * 128],
                                s_zbf[:, dg * CH2:(dg + 1) * CH2],
                                start=(dg == 0), stop=(dg == 15)))
                        wdone[wi2] = last
                    E.wait("scalar", "p", E.cnt["p"])
                    E.wait("vector", "p", E.cnt["p"])
                    for k, hc in enumerate(hcs):
                        if part == "g":
                            E.act(lambda s, k=k, hc=hc: s.activation(
                                out=s_hid[:, hc * CH2:(hc + 1) * CH2],
                                in_=p_mm[:, k * CH2:(k + 1) * CH2], func=ACTF.Silu,
                                scale=s_sg[:, hc:hc + 1]))
                        else:
                            E.act(lambda s, k=k, hc=hc: s.activation(
                                out=s_ut[:], in_=p_mm[:, k * CH2:(k + 1) * CH2],
                                func=ACTF.Copy, scale=s_su[:, hc:hc + 1]))
                            E.wait("vector", "a", E.cnt["a"])
                            E.vec(lambda v, hc=hc: v.tensor_mul(
                                s_hid[:, hc * CH2:(hc + 1) * CH2],
                                s_hid[:, hc * CH2:(hc + 1) * CH2], s_ut[:]))
            E.wait("tensor", "v", E.cnt["v"])
            E.wait("tensor", "a", E.cnt["a"])
            for dcg in range(2):
                for kc in range(HC):
                    wi2 = widx[0]
                    widx[0] += 1
                    wi = s_wi[:, (wi2 % 2) * 1024:((wi2 % 2) + 1) * 1024]
                    wib = s_wib[:, (wi2 % 2) * 1024:((wi2 % 2) + 1) * 1024]
                    if wi2 >= 2:
                        E.wait("sync", "p", wdone[wi2 - 2])
                    E.dma("sync", wi, WdT[kc * 128:(kc + 1) * 128,
                                          dcg * 1024:(dcg + 1) * 1024])
                    E.wait("scalar", "q", E.cnt["q"])
                    E.act(lambda s, wib=wib, wi=wi: s.activation(
                        out=wib, in_=wi, func=ACTF.Copy, bias=-128.0))
                    E.wait("tensor", "a", E.cnt["a"])
                    if kc == 0:
                        E.wait("tensor", "v", E.cnt["v"])
                    for dc in range(8):
                        last = E.pe(lambda s, wib=wib, dc=dc, kc=kc: s.matmul(
                            p_mm[:, dc * CH2:(dc + 1) * CH2],
                            wib[:, dc * 128:(dc + 1) * 128],
                            s_hid[:, kc * CH2:(kc + 1) * CH2],
                            start=(kc == 0), stop=(kc == HC - 1)))
                    wdone[wi2] = last
                E.wait("vector", "p", E.cnt["p"])
                for dc in range(8):
                    dgl = dcg * 8 + dc
                    E.vec(lambda v, dc=dc, dgl=dgl: v.scalar_tensor_tensor(
                        out=s_z[:, dgl * CH2:(dgl + 1) * CH2],
                        in0=p_mm[:, dc * CH2:(dc + 1) * CH2],
                        scalar=s_sd[:, dgl:dgl + 1],
                        in1=s_z[:, dgl * CH2:(dgl + 1) * CH2],
                        op0=ALU.mult, op1=ALU.add))
                E.wait("tensor", "v", E.cnt["v"])
            E.wait("scalar", "v", E.cnt["v"])
            for dg in range(16):
                E.wait("scalar", "p", E.cnt["p"])
                E.act(lambda s, dg=dg: s.activation(
                    out=s_zbf[:, dg * CH2:(dg + 1) * CH2],
                    in_=s_z[:, dg * CH2:(dg + 1) * CH2], func=ACTF.Copy))
                E.act(lambda s, dg=dg: s.activation(
                    out=s_sq[:], in_=s_z[:, dg * CH2:(dg + 1) * CH2], func=ACTF.Square))
                E.wait("tensor", "a", E.cnt["a"])
                E.pe(lambda s, dg=dg: s.matmul(p_mm[0:1, 0:CH2], s_ones[:],
                                               s_zbf[:, dg * CH2:(dg + 1) * CH2],
                                               start=(dg == 0), stop=(dg == 15)))
                E.pe(lambda s, dg=dg: s.matmul(p_mm[0:1, CH2:2 * CH2], s_ones[:], s_sq[:],
                                               start=(dg == 0), stop=(dg == 15)))
            E.wait("scalar", "p", E.cnt["p"])
            ln_rows(p_mm[0:1, 0:CH2], p_mm[0:1, CH2:2 * CH2],
                    p_mm[:, 2 * CH2:3 * CH2], p_mm[:, 3 * CH2:4 * CH2])
            for dg in range(16):
                oi = c * 16 + dg
                ob = s_ob[:, (oi % 2) * CH2:((oi % 2) + 1) * CH2]
                if oi >= 2:
                    E.wait("vector", "q2", odma[oi - 2])
                E.vec(lambda v, dg=dg: v.tensor_mul(
                    s_m1[:], s_z[:, dg * CH2:(dg + 1) * CH2], s_rbc[:]))
                E.vec(lambda v, ob=ob: v.tensor_sub(ob, s_m1[:], s_mrbc[:]))
                E.wait("scalar", "v", E.cnt["v"])
                odma[oi] = E.dma("scalar", out[dg * 128:(dg + 1) * 128, col], ob, sem="q2")

        E.wait("gpsimd", "q", E.cnt["q"])
        E.wait("gpsimd", "q2", E.cnt["q2"])
    return nc



def kernel(L0_attn_x, L0_attn_norm_w, L0_attn_norm_b, L0_attn_x_proj_w,
           L0_attn_dt_proj_w, L0_attn_dt_proj_b, L0_attn_A_log, L0_attn_D_param,
           L0_attn_out_proj_w, L0_ffn_wq_gate, L0_ffn_s_gate, L0_ffn_wq_up,
           L0_ffn_s_up, L0_ffn_wq_down, L0_ffn_s_down):
    c = np.ascontiguousarray
    xT = c(np.asarray(L0_attn_x, np.float32)[0].T)
    PT = c(np.asarray(L0_attn_x_proj_w, np.float32).T)
    nw = c(np.asarray(L0_attn_norm_w, np.float32)[:, None])
    nb = c(np.asarray(L0_attn_norm_b, np.float32)[:, None])
    dtw = np.asarray(L0_attn_dt_proj_w, np.float32)
    dtbf = np.asarray(L0_attn_dt_proj_b, np.float32)
    Alog = np.asarray(L0_attn_A_log, np.float32)

    nc1 = build_scan()
    maps1 = []
    for k in range(NC):
        sl = slice(k * DSL, (k + 1) * DSL)
        maps1.append(dict(
            xT=xT, xTo=c(xT[sl]), PT=PT, nw=nw, nb=nb,
            WdtT=c(dtw[sl].T), dtb=c(dtbf[sl][:, None]), Ao=c(Alog[sl]),
            nwo=c(nw[sl]), nbo=c(nb[sl])))
    res1 = run_bass_kernel_spmd(nc1, maps1, core_ids=list(range(NC)))
    ys = []
    for r in res1.results:
        a = np.asarray(r["y_out"])
        if a.dtype != BF:
            a = a.view(BF)
        ys.append(a)
    y_full = np.concatenate(ys, axis=0)

    WoT = c(np.asarray(L0_attn_out_proj_w, np.float32).T)
    WgT = c(np.asarray(L0_ffn_wq_gate, np.int32).T)
    WuT = c(np.asarray(L0_ffn_wq_up, np.int32).T)
    WdT = c(np.asarray(L0_ffn_wq_down, np.int32).T)
    sg = c(np.asarray(L0_ffn_s_gate, np.float32)[:, None])
    su = c(np.asarray(L0_ffn_s_up, np.float32)[:, None])
    sd = c(np.asarray(L0_ffn_s_down, np.float32)[:, None])
    Dp = c(np.asarray(L0_attn_D_param, np.float32)[:, None])

    nc2 = build_ffn()
    maps2 = []
    for k in range(NC):
        ts = slice(k * TL, (k + 1) * TL)
        maps2.append(dict(
            yT=c(y_full[:, ts]), xTt=c(xT[:, ts]), Dp=Dp, WoT=WoT,
            WgT=WgT, WuT=WuT, WdT=WdT, sg=sg, su=su, sd=sd))
    res2 = run_bass_kernel_spmd(nc2, maps2, core_ids=list(range(NC)))
    out = np.concatenate([np.asarray(r["out"], np.float32).T for r in res2.results],
                         axis=0)
    return out[None, :, :]
